# version 03
# baseline (speedup 1.0000x reference)
import numpy as np
import ml_dtypes
from contextlib import ExitStack

import concourse.bass as bass
import concourse.mybir as mybir
from concourse.bass_utils import run_bass_kernel_spmd

F32 = mybir.dt.float32
BF16 = mybir.dt.bfloat16
U8 = mybir.dt.uint8
AF = mybir.ActivationFunctionType
ALU = mybir.AluOpType
AXX = mybir.AxisListType.X

NCORES = 8
B = 16
BPC = B // NCORES
T = 2048
D = 2048
TOK = BPC * T
I_FFN = 8192
CH3 = 6144
bf16 = ml_dtypes.bfloat16


def _u8T(w_int32):
    u8 = np.ascontiguousarray(w_int32).view(np.uint8)[:, 0::4]
    return np.ascontiguousarray(u8.T)

def _scale128(v):
    return np.ascontiguousarray(np.asarray(v).reshape(-1, 128).T)

def _dup_pergroup(v):
    v = np.asarray(v)
    ng = v.shape[0] // 8
    out = np.empty((128, ng), dtype=np.float32)
    for c in range(8):
        out[c * 16:(c + 1) * 16, :] = v.reshape(ng, 8)[:, c][None, :]
    return out

def _build_mshort(short_w):
    short_w = np.asarray(short_w)
    ng = CH3 // 8
    M = np.zeros((ng, 128, 128), dtype=np.float32)
    for c in range(8):
        for i in range(16):
            for j in range(max(0, i - 2), i + 1):
                M[:, c * 16 + j, c * 16 + i] = short_w[np.arange(ng) * 8 + c, j - i + 2]
    return M

def _build_mlong(k32):
    ng = D // 8
    M = np.zeros((ng, 128, 128), dtype=np.float32)
    for c in range(8):
        for i in range(16):
            for j in range(16):
                M[:, c * 16 + j, c * 16 + i] = k32[np.arange(ng) * 8 + c, (i - j) % 32]
    return M

def _rope_tables():
    inv = (1.0 / 10000.0) ** (np.arange(0, 128, 2, dtype=np.float64) / 128.0)
    f = np.outer(np.arange(T, dtype=np.float64), inv)
    return np.cos(f).astype(np.float32), np.sin(f).astype(np.float32)

def _rope128(tab):
    return np.ascontiguousarray(tab.reshape(16, 128, 64).transpose(1, 0, 2).reshape(128, 1024))

def prep_host(inp):
    f = np.ascontiguousarray
    h = {}
    h["a_wqT"] = f(np.asarray(inp["L0_attn_a_wq"]).T)
    h["a_wkT"] = f(np.asarray(inp["L0_attn_a_wk"]).T)
    h["a_wvT"] = f(np.asarray(inp["L0_attn_a_wv"]).T)
    h["a_woT"] = f(np.asarray(inp["L0_attn_a_wo"]).T)
    for n in ("lq1", "lk1", "lq2", "lk2"):
        h["a_" + n] = f(np.asarray(inp["L0_attn_a_" + n]).reshape(1, 256))
    h["b_wq"] = f(inp["L0_attn_b_Wq"]); h["b_wk"] = f(inp["L0_attn_b_Wk"])
    h["b_wv"] = f(inp["L0_attn_b_Wv"]); h["b_wo"] = f(inp["L0_attn_b_Wo"])
    for L in (0, 1):
        p = f"L{L}_ffn_"
        h[f"f{L}_g8"] = _u8T(np.asarray(inp[p + "gate_w"]))
        h[f"f{L}_u8"] = _u8T(np.asarray(inp[p + "up_w"]))
        h[f"f{L}_d8"] = _u8T(np.asarray(inp[p + "down_w"]))
        h[f"f{L}_gsc"] = _scale128(inp[p + "gate_scale"])
        h[f"f{L}_usc"] = _scale128(inp[p + "up_scale"])
        h[f"f{L}_dsc"] = f(np.asarray(inp[p + "down_scale"]).reshape(1, D))
    h["h_lng"] = f(np.asarray(inp["L1_attn_a_ln_gamma"]).reshape(1, D))
    h["h_lnb"] = f(np.asarray(inp["L1_attn_a_ln_beta"]).reshape(1, D))
    h["h_winT"] = f(np.asarray(inp["L1_attn_a_W_in"]).T)
    h["h_bin"] = _scale128(inp["L1_attn_a_b_in"])
    h["h_mshort"] = _build_mshort(inp["L1_attn_a_short_w"])
    h["h_sbdup"] = _dup_pergroup(inp["L1_attn_a_short_b"])
    h["h_mlong"] = _build_mlong(f(np.asarray(inp["L1_attn_a_k"])[:, :32]))
    h["h_cbdup"] = _dup_pergroup(inp["L1_attn_a_conv_bias"])
    h["h_woutT"] = f(np.asarray(inp["L1_attn_a_W_out"]).T)
    h["c_wq"] = f(inp["L1_attn_b_wq"]); h["c_wk"] = f(inp["L1_attn_b_wk"])
    h["c_wv"] = f(inp["L1_attn_b_wv"]); h["c_wo"] = f(inp["L1_attn_b_wo"])
    cos, sin = _rope_tables()
    h["ropec"] = _rope128(cos); h["ropes"] = _rope128(sin)
    kk = np.arange(128)[:, None]; qq = np.arange(128)[None, :]
    h["cslo"] = (kk <= qq).astype(np.float32).astype(bf16)
    h["cshi"] = (kk > qq).astype(np.float32).astype(bf16)
    ind = np.zeros((128, 256), dtype=np.float32)
    for kt in range(16):
        ind[:, kt * 16 + kt] = 1.0 / 128.0
    h["ind"] = ind
    h["ident"] = np.eye(128, dtype=np.float32)
    h["identb"] = np.eye(128, dtype=bf16)
    invf = (1.0 / 10000.0) ** (np.arange(0, 128, 2, dtype=np.float64) / 128.0)
    f8 = np.outer(np.arange(8, dtype=np.float64), invf)
    h["a_cos8"] = np.broadcast_to(np.cos(f8).reshape(1, 512), (128, 512)).astype(np.float32)
    h["a_sin8"] = np.broadcast_to(np.sin(f8).reshape(1, 512), (128, 512)).astype(np.float32)
    return {k2: np.ascontiguousarray(v) for k2, v in h.items()}

HOST_SPECS = {
    "a_wqT": ([D, D], F32), "a_wkT": ([D, D], F32), "a_wvT": ([D, D], F32),
    "a_woT": ([D, D], F32),
    "a_lq1": ([1, 256], F32), "a_lk1": ([1, 256], F32),
    "a_lq2": ([1, 256], F32), "a_lk2": ([1, 256], F32),
    "b_wq": ([D, D], F32), "b_wk": ([D, D], F32), "b_wv": ([D, D], F32),
    "b_wo": ([D, D], F32),
    "f0_g8": ([D, I_FFN], U8), "f0_u8": ([D, I_FFN], U8), "f0_d8": ([I_FFN, D], U8),
    "f0_gsc": ([128, 64], F32), "f0_usc": ([128, 64], F32), "f0_dsc": ([1, D], F32),
    "f1_g8": ([D, I_FFN], U8), "f1_u8": ([D, I_FFN], U8), "f1_d8": ([I_FFN, D], U8),
    "f1_gsc": ([128, 64], F32), "f1_usc": ([128, 64], F32), "f1_dsc": ([1, D], F32),
    "h_lng": ([1, D], F32), "h_lnb": ([1, D], F32),
    "h_winT": ([D, CH3], F32), "h_bin": ([128, 48], F32),
    "h_mshort": ([768, 128, 128], F32), "h_sbdup": ([128, 768], F32),
    "h_mlong": ([256, 128, 128], F32), "h_cbdup": ([128, 256], F32),
    "h_woutT": ([D, D], F32),
    "c_wq": ([D, D], F32), "c_wk": ([D, 1024], F32), "c_wv": ([D, 1024], F32),
    "c_wo": ([D, D], F32),
    "ropec": ([128, 1024], F32), "ropes": ([128, 1024], F32),
    "cslo": ([128, 128], BF16), "cshi": ([128, 128], BF16),
    "ind": ([128, 256], F32),
    "ident": ([128, 128], F32), "identb": ([128, 128], BF16),
    "a_cos8": ([128, 512], F32), "a_sin8": ([128, 512], F32),
}

ABF_N = 75776
A_FM = 0
A_H = 8192
A_WG = 40960
A_WU = 45056
A_WD = 49152
A_SILU = 50176
A_SQ = 51200
A_TM = 53248
A_FREE = 55296
A_GBETA = 73216
A_ONESB = 75264
A_CSHI = 75392
A_CSLO = 75520
A_IDB = 75648

AF32_N = 11264
X_X = 0
X_BC = 8192
X_FREE = 10240

ASM_N = 3840
S_GSC = 0
S_USC = 64
S_STAT = 128
S_ONES1 = 192
S_SC2 = 320
S_ROPEC = 1024
S_ROPES = 2048
S_IND = 3072
S_IDENT = 3328
S_MISC = 3456
S_LAM = 3456
S_SCL = 3457
S_B12 = 3458
S_B24 = 3459
S_RST = 3464
S_MISC2 = 3472

AT_STG = 0
AT_PRE = 1536
AT_SQ2 = 2560
AT_XN = 8192
AT_W = 40960
AT_QF = 49152
AT_KF = 57344
AT_PU = 65536
AT_VA = 66944

ENGS = ("sync", "gpsimd", "tensor", "vector", "scalar")


class Sem:
    def __init__(self, es, nc, name):
        self.h = es.enter_context(nc.semaphore(name))
        self.v = 0


CONST_NAMES = {"ropec", "ropes", "cslo", "cshi", "ind", "ident", "identb"}


class Ctx:
    def __init__(self, es, need=None):
        if need is not None:
            need = set(need) | CONST_NAMES
        self.es = es
        nc = bass.Bass("TRN2", target_bir_lowering=False)
        self.nc = nc
        E = es.enter_context
        self.E = E
        self.hh = {n: nc.dram_tensor(n, s, d, kind="ExternalInput").ap()
                   for n, (s, d) in HOST_SPECS.items()
                   if need is None or n in need}
        self.wscr = {}
        for L in (0, 1):
            self.wscr[f"g{L}"] = nc.dram_tensor(f"wscr_g{L}", [D, I_FFN], BF16).ap()
            self.wscr[f"u{L}"] = nc.dram_tensor(f"wscr_u{L}", [D, I_FFN], BF16).ap()
            self.wscr[f"d{L}"] = nc.dram_tensor(f"wscr_d{L}", [I_FFN, D], BF16).ap()
        self.abf = E(nc.sbuf_tensor("abf", [128, ABF_N], BF16))
        self.af32 = E(nc.sbuf_tensor("af32", [128, AF32_N], F32))
        self.asm = E(nc.sbuf_tensor("asm", [128, ASM_N], F32))
        self.aout = nc.dram_tensor("aout_scr", [T, D], BF16).ap()
        self.upb = nc.dram_tensor("upb_scr", [CH3, 4096], BF16).ap()
        self.yb = nc.dram_tensor("yb_scr", [D, 4096], BF16).ap()
        self.p32 = [E(nc.psum_tensor(f"p32_{i}", [128, 512], F32)) for i in range(6)]
        self.pbf = [E(nc.psum_tensor(f"pbf_{i}", [128, 1024], BF16)) for i in range(2)]
        for s in ("sLD", "sG", "sPE", "sDV", "sSC", "sST", "sCC", "sK", "sX",
                  "sWa", "sWb", "sGa", "sGb", "sSa", "sSb", "sXa", "sXb"):
            setattr(self, s, Sem(es, nc, s))
        self.sW = [self.sWa, self.sWb]
        self.sGp = [self.sGa, self.sGb]
        self.sS = [self.sSa, self.sSb]
        self.sXp = [self.sXa, self.sXb]
        self.consts_ld = 0

    def bf(self, off, n, p0=0, p=128):
        assert 0 <= off and off + n <= ABF_N, (off, n)
        return self.abf[p0:p0 + p, off:off + n]

    def f32(self, off, n, p0=0, p=128):
        assert 0 <= off and off + n <= AF32_N, (off, n)
        return self.af32[p0:p0 + p, off:off + n]

    def sm(self, off, n, p0=0, p=128):
        assert 0 <= off and off + n <= ASM_N, (off, n)
        return self.asm[p0:p0 + p, off:off + n]


class P:

    def __init__(self, c):
        self.c = c
        self.q = {e: [] for e in ENGS}
        self._lastwait = {e: {} for e in ENGS}

    def _add(self, eng, fn, wait=(), inc=None, incn=1):
        waits = []
        lw = self._lastwait[eng]
        for sem, val in wait:
            if val > 0 and lw.get(id(sem), -1) < val:
                waits.append((sem, val))
                lw[id(sem)] = val
        ret = None
        if inc is not None:
            inc.v += incn
            ret = inc.v
        self.q[eng].append((waits, fn, inc, incn))
        return ret

    def dma(self, eng, out, in_, wait=(), sem=None):
        sem = sem or self.c.sLD
        return self._add(eng, lambda h, o=out, i=in_: h.dma_start(out=o, in_=i),
                         wait, sem, 16)

    def mm(self, out, lhsT, rhs, start, stop, wait=(), inc=None):
        return self._add("tensor",
                         lambda h, o=out, l=lhsT, r=rhs, a=start, b=stop:
                         h.matmul(o, l, r, start=a, stop=b), wait, inc, 1)

    def trans(self, out, in_, ident, wait=(), inc=None):
        return self._add("tensor",
                         lambda h, o=out, i=in_, d=ident: h.transpose(o, i, d),
                         wait, inc, 1)

    def act(self, out, in_, func, bias=None, scale=None, accum=None, wait=(), inc=None):
        def fn(h, o=out, i=in_, f=func, b=bias, s=scale, a=accum):
            kw = {}
            if b is not None: kw["bias"] = b
            if s is not None: kw["scale"] = s
            if a is not None: kw["accum_out"] = a
            return h.activation(o, i, f, **kw)
        return self._add("scalar", fn, wait, inc, 1)

    def v(self, name, *args, wait=(), inc=None, eng="vector"):
        if inc is None and eng == "vector":
            inc = self.c.sDV
        return self._add(eng, lambda h, n=name, a=args: getattr(h, n)(*a),
                         wait, inc, 1)

    def cc(self, kind, op, groups, ins, outs, wait=(), inc=None):
        return self._add("gpsimd",
                         lambda h, k=kind, o=op, g=groups, ii=ins, oo=outs:
                         h.collective_compute(k, o, replica_groups=g, ins=ii, outs=oo),
                         wait, inc, 1)

    def flush(self):
        with self.c.nc.Block() as blk:
            for eng in ENGS:
                items = self.q[eng]
                if not items:
                    continue
                def body(h, items=items):
                    for waits, fn, inc, incn in items:
                        for sem, val in waits:
                            h.wait_ge(sem.h, val)
                        ins = fn(h)
                        if inc is not None:
                            ins.then_inc(inc.h, incn)
                getattr(blk, eng)(body)
        self.q = {e: [] for e in ENGS}
        self._lastwait = {e: {} for e in ENGS}


def W(sem):
    return (sem, sem.v)


def WST(c):
    return [W(c.sST), W(c.sSa), W(c.sSb)]



def emit_consts(p):
    c = p.c
    p.dma("sync", c.sm(S_ROPEC, 1024), c.hh["ropec"], sem=c.sK)
    p.dma("sync", c.sm(S_ROPES, 1024), c.hh["ropes"], sem=c.sK)
    p.dma("sync", c.sm(S_IND, 256), c.hh["ind"], sem=c.sK)
    p.dma("sync", c.sm(S_IDENT, 128), c.hh["ident"], sem=c.sK)
    p.dma("sync", c.bf(A_CSLO, 128), c.hh["cslo"], sem=c.sK)
    p.dma("sync", c.bf(A_CSHI, 128), c.hh["cshi"], sem=c.sK)
    p.dma("sync", c.bf(A_IDB, 128), c.hh["identb"], sem=c.sK)
    c.consts_ld = c.sK.v
    p.v("memset", c.sm(S_ONES1, 128, p=1), 1.0)
    p.v("memset", c.sm(S_SCL, 1), 1.0 / float(np.sqrt(128.0)))
    p.v("memset", c.sm(S_B12, 1), -12.0)
    p.v("memset", c.sm(S_B24, 1), -24.0)
    p.v("memset", c.bf(A_ONESB, 128, p=1), 1.0, inc=c.sDV)


def emit_bcast_row(p, src_row_f32, dst_f32, width, wait=()):
    c = p.c
    out = None
    for i in range(0, width, 512):
        w = min(512, width - i)
        mmv = p.mm(c.p32[4][:, 0:w], c.sm(S_ONES1, 128, p=1),
                   src_row_f32[0:1, i:i + w], start=True, stop=True,
                   wait=list(wait) + [W(c.sDV)], inc=c.sPE)
        out = p.v("tensor_copy", dst_f32[:, i:i + w], c.p32[4][:, 0:w],
                  wait=[(c.sPE, mmv)], inc=c.sDV)
    return out


def emit_ffn_dequant(p, L):
    c = p.c
    ld = p.dma("sync", c.f32(X_BC, 2048, p=1), c.hh[f"f{L}_dsc"], wait=WST(c))
    emit_bcast_row(p, c.f32(X_BC, 2048, p=1), c.f32(X_BC, 2048), 2048,
                   wait=[(c.sLD, ld)])
    bc_done = c.sDV.v
    dq = [(c.hh[f"f{L}_g8"], c.wscr[f"g{L}"], 16, I_FFN, False),
          (c.hh[f"f{L}_u8"], c.wscr[f"u{L}"], 16, I_FFN, False),
          (c.hh[f"f{L}_d8"], c.wscr[f"d{L}"], 64, D, True)]
    stv = {}
    n = 0
    for src, dst, ntile, width, is_down in dq:
        for i in range(ntile):
            par = n % 2
            boff = A_H + par * 8192
            w = []
            if n >= 2:
                w.append((c.sS[par], stv[n - 2]))
            ldv = p.dma("gpsimd", c.bf(boff, width), src[i * 128:(i + 1) * 128, :],
                        wait=w, sem=c.sGp[par])
            dv = p.v("tensor_scalar_add", c.bf(boff, width), c.bf(boff, width),
                     -128.0, wait=[(c.sGp[par], ldv)], inc=c.sDV)
            if is_down:
                dv = p.v("tensor_mul", c.bf(boff, width), c.bf(boff, width),
                         c.f32(X_BC, 2048), wait=[(c.sDV, bc_done)], inc=c.sDV)
            stv[n] = p.dma("sync", dst[i * 128:(i + 1) * 128, :], c.bf(boff, width),
                           wait=[(c.sDV, dv)], sem=c.sS[par])
            n += 1
    c.ffn_dq_store = (c.sSa.v, c.sSb.v)


def emit_ln_chunk(p, x_src, row0, nt, delta_src=None, xeff_dst=None, inner=False,
                  fm_off=A_FM, fm_stride=None, col0=0, tr_free=None):
    c = p.c
    ctok = nt * 128
    if fm_stride is None:
        fm_stride = ctok
    if tr_free is None:
        tr_free = {}
    for t in range(nt):
        xoff = X_X + t * 2048
        st = S_STAT + t * 8
        ldv = p.dma("sync", c.f32(xoff, 2048),
                    x_src[row0 + t * 128: row0 + t * 128 + 128, :],
                    wait=[W(c.sDV), W(c.sSC)] + WST(c), sem=c.sX)
        addv = None
        if delta_src is not None:
            ldv = p.dma("sync", c.f32(X_BC, 2048),
                        delta_src[row0 + t * 128: row0 + t * 128 + 128, :], sem=c.sX)
            addv = p.v("tensor_add", c.f32(xoff, 2048), c.f32(xoff, 2048),
                       c.f32(X_BC, 2048), wait=[(c.sX, ldv)], inc=c.sDV)
            if xeff_dst is not None:
                p.dma("sync", xeff_dst[row0 + t * 128: row0 + t * 128 + 128, :],
                      c.f32(xoff, 2048), wait=[(c.sDV, addv)], sem=c.sST)
        sw0 = [(c.sX, ldv)]
        if addv is not None:
            sw0.append((c.sDV, addv))
        sumv = p.v("reduce_sum", c.sm(st, 1), c.f32(xoff, 2048), AXX, wait=sw0)
        nmv = p.v("tensor_scalar_mul", c.sm(st + 2, 1), c.sm(st, 1), -1.0 / 2048.0,
                  wait=[(c.sDV, sumv)])
        sqv = p.act(c.bf(A_SQ, 2048), c.f32(xoff, 2048), AF.Square,
                    bias=c.sm(st + 2, 1), accum=c.sm(st + 1, 1),
                    wait=sw0 + [(c.sDV, nmv)], inc=c.sSC)
        varv = p.v("tensor_scalar", c.sm(st + 3, 1), c.sm(st + 1, 1), 1.0 / 2048.0,
                   1e-5, ALU.mult, ALU.add, wait=[(c.sSC, sqv)])
        sqrtv = p.act(c.sm(st + 5, 1), c.sm(st + 3, 1), AF.Sqrt,
                      wait=[(c.sDV, varv)], inc=c.sSC)
        recv = p.v("reciprocal", c.sm(st + 6, 1), c.sm(st + 5, 1), wait=[(c.sSC, sqrtv)])
        tmv = p.v("tensor_scalar", c.bf(A_TM, 2048), c.f32(xoff, 2048),
                  c.sm(st + 2, 1), c.sm(st + 6, 1), ALU.add, ALU.mult,
                  wait=[(c.sDV, recv)])
        if inner:
            tmv = emit_inner_ln(p, st)
        emit_transpose_tm(p, A_TM, fm_off + col0 + t * 128, fm_stride, 16, tmv, tr_free)
    return tr_free


def emit_transpose_tm(p, src_off, dst_off, dst_stride, ntiles, ready_dv, tr_free,
                      evac_wait=()):
    c = p.c
    ngroups = (ntiles + 3) // 4
    for g in range(ngroups):
        bank = g % 2
        n4 = min(4, ntiles - g * 4)
        w = [(c.sDV, ready_dv), (c.sK, c.consts_ld)]
        if bank in tr_free:
            w.append((c.sDV, tr_free[bank]))
        trv = None
        for s4 in range(n4):
            k = g * 4 + s4
            trv = p.trans(c.pbf[bank][:, s4 * 128:(s4 + 1) * 128],
                          c.bf(src_off + k * 128, 128), c.bf(A_IDB, 128),
                          wait=w if s4 == 0 else (),
                          inc=c.sPE if s4 == n4 - 1 else None)
        src = bass.AP(c.pbf[bank].ap().tensor, 0, [[1024, 128], [128, n4], [1, 128]])
        dst = bass.AP(c.abf.ap().tensor, dst_off + (g * 4) * dst_stride,
                      [[ABF_N, 128], [dst_stride, n4], [1, 128]])
        tr_free[bank] = p.v("tensor_copy", dst, src,
                            wait=[(c.sPE, trv)] + list(evac_wait), inc=c.sDV)
    return tr_free


def emit_inner_ln(p, st):
    c = p.c
    tm_ready = c.sDV.v
    s2 = p.v("reduce_sum", c.sm(st + 7, 1), c.bf(A_TM, 2048), AXX)
    nm2 = p.v("tensor_scalar_mul", c.sm(st + 7, 1), c.sm(st + 7, 1), -1.0 / 2048.0,
              wait=[(c.sDV, s2)])
    sq2 = p.act(c.bf(A_SQ, 2048), c.bf(A_TM, 2048), AF.Square,
                bias=c.sm(st + 7, 1), accum=c.sm(S_SC2, 1),
                wait=[(c.sDV, nm2)], inc=c.sSC)
    v2 = p.v("tensor_scalar", c.sm(S_SC2 + 1, 1), c.sm(S_SC2, 1), 1.0 / 2048.0, 1e-5,
             ALU.mult, ALU.add, wait=[(c.sSC, sq2)])
    sq2b = p.act(c.sm(S_SC2 + 3, 1), c.sm(S_SC2 + 1, 1), AF.Sqrt,
                 wait=[(c.sDV, v2)], inc=c.sSC)
    r2 = p.v("reciprocal", c.sm(S_SC2 + 4, 1), c.sm(S_SC2 + 3, 1), wait=[(c.sSC, sq2b)])
    a1 = p.v("tensor_scalar", c.bf(A_TM, 2048), c.bf(A_TM, 2048), c.sm(st + 7, 1),
             c.sm(S_SC2 + 4, 1), ALU.add, ALU.mult, wait=[(c.sDV, r2)])
    a2 = p.v("tensor_mul", c.bf(A_TM, 2048), c.bf(A_TM, 2048), c.f32(X_BC, 2048),
             wait=[(c.sDV, a1)])
    return p.v("tensor_add", c.bf(A_TM, 2048), c.bf(A_TM, 2048),
               c.bf(A_GBETA, 2048), wait=[(c.sDV, a2)])


def emit_ffn_call(p, L, x_src, x_dst, delta_src=None, xeff_dst=None, dbg=None):
    c = p.c
    gscr, uscr, dscr = c.wscr[f"g{L}"], c.wscr[f"u{L}"], c.wscr[f"d{L}"]
    p.dma("sync", c.sm(S_GSC, 64), c.hh[f"f{L}_gsc"], wait=[W(c.sSC), W(c.sDV)])
    sc_ld = p.dma("sync", c.sm(S_USC, 64), c.hh[f"f{L}_usc"])
    dqa, dqb = getattr(c, "ffn_dq_store", (0, 0))
    for chunk in range(8):
        row0 = chunk * 512
        emit_ln_chunk(p, x_src, row0, 4, delta_src=delta_src, xeff_dst=xeff_dst)
        xn_ready = c.sDV.v
        if dbg is not None and chunk == 0:
            p.dma("sync", dbg["xn0"], c.bf(A_FM, 8192), wait=[(c.sDV, xn_ready)], sem=c.sST)
            p.dma("sync", dbg["st0"], c.sm(0, 512), wait=[(c.sDV, xn_ready)], sem=c.sST)
            p.dma("sync", dbg["xt0"], c.f32(X_X, 8192), wait=[(c.sDV, xn_ready)], sem=c.sST)
        mm_done = {}
        evac_done = {}
        for i in range(64):
            bs = i % 2
            wwait = [(c.sPE, mm_done[i - 2])] if i >= 2 else []
            wwait += [(c.sSa, dqa), (c.sSb, dqb)]
            src_g = bass.AP(gscr.tensor, i * 128,
                            [[I_FFN, 128], [I_FFN * 128, 16], [1, 128]])
            src_u = bass.AP(uscr.tensor, i * 128,
                            [[I_FFN, 128], [I_FFN * 128, 16], [1, 128]])
            dst_g = bass.AP(c.abf.ap().tensor, A_WG + bs * 2048,
                            [[ABF_N, 128], [128, 16], [1, 128]])
            dst_u = bass.AP(c.abf.ap().tensor, A_WU + bs * 2048,
                            [[ABF_N, 128], [128, 16], [1, 128]])
            p.dma("sync", dst_g, src_g, wait=wwait, sem=c.sW[bs])
            wld = p.dma("sync", dst_u, src_u, sem=c.sW[bs])
            pg, pu = c.p32[2 * bs], c.p32[2 * bs + 1]
            w0 = [(c.sW[bs], wld), (c.sDV, xn_ready)]
            if i >= 2:
                w0.append((c.sDV, evac_done[i - 2]))
            for k in range(16):
                p.mm(pg[:, 0:512], c.bf(A_WG + bs * 2048 + k * 128, 128),
                     c.bf(A_FM + k * 512, 512), start=(k == 0), stop=(k == 15),
                     wait=w0 if k == 0 else ())
            mmv = None
            for k in range(16):
                mmv = p.mm(pu[:, 0:512], c.bf(A_WU + bs * 2048 + k * 128, 128),
                           c.bf(A_FM + k * 512, 512), start=(k == 0), stop=(k == 15),
                           inc=c.sPE if k == 15 else None)
            mm_done[i] = mmv
            sw = [(c.sPE, mmv), (c.sLD, sc_ld)]
            if i >= 2:
                sw.append((c.sDV, evac_done[i - 2]))
            sv = p.act(c.bf(A_SILU + bs * 512, 512), pg[:, 0:512], AF.Silu,
                       scale=c.sm(S_GSC + i, 1), wait=sw, inc=c.sSC)
            ew = [(c.sSC, sv)]
            if chunk == 0 and i == 0:
                ew += [(c.sSa, dqa), (c.sSb, dqb)]
            evac_done[i] = p.v("scalar_tensor_tensor", c.bf(A_H + i * 512, 512),
                               pu[:, 0:512], c.sm(S_USC + i, 1),
                               c.bf(A_SILU + bs * 512, 512), ALU.mult, ALU.mult,
                               wait=ew, inc=c.sDV)
        if dbg is not None and chunk == 0:
            p.dma("sync", dbg["h0"], c.bf(A_H, 32768), wait=[(c.sDV, evac_done[63])], sem=c.sST)
        dn_ev = None
        for dpass in range(4):
            dmm = {}
            for i in range(64):
                bs = i % 2
                wwait = [(c.sSa, dqa), (c.sSb, dqb)]
                if i >= 2:
                    wwait.append((c.sPE, dmm[i - 2]))
                elif dpass > 0:
                    wwait.append((c.sPE, prev_pass_pe))
                dwld = p.dma("sync", c.bf(A_WD + bs * 512, 512),
                             dscr[i * 128:(i + 1) * 128, dpass * 512:(dpass + 1) * 512],
                             wait=wwait, sem=c.sW[bs])
                w0 = [(c.sW[bs], dwld), (c.sDV, evac_done[63])]
                if dpass > 0 and i == 0:
                    w0.append((c.sDV, dn_ev))
                for tt in range(4):
                    dmm[i] = p.mm(c.p32[tt][:, 0:512],
                                  c.bf(A_H + i * 512 + tt * 128, 128),
                                  c.bf(A_WD + bs * 512, 512),
                                  start=(i == 0), stop=(i == 63),
                                  wait=w0 if tt == 0 else (),
                                  inc=c.sPE if tt == 3 else None)
            prev_pass_pe = dmm[63]
            for tt in range(4):
                dn_ev = p.v("tensor_add",
                            c.f32(X_X + tt * 2048 + dpass * 512, 512),
                            c.p32[tt][:, 0:512],
                            c.f32(X_X + tt * 2048 + dpass * 512, 512),
                            wait=[(c.sPE, prev_pass_pe)], inc=c.sDV)
        for tt in range(4):
            p.dma("sync", x_dst[row0 + tt * 128: row0 + tt * 128 + 128, :],
                  c.f32(X_X + tt * 2048, 2048), wait=[(c.sDV, dn_ev)], sem=c.sST)


def emit_final_wait(p):
    c = p.c
    p.v("memset", c.sm(S_MISC, 1), 0.0, wait=WST(c))





def emit_lam(p):
    c = p.c
    ld = None
    for i, n in enumerate(("a_lq1", "a_lk1", "a_lq2", "a_lk2")):
        ld = p.dma("sync", c.f32(X_FREE + i * 256, 256, p=1), c.hh[n],
                   wait=[W(c.sDV)] if i == 0 else ())
    t1 = p.v("tensor_mul", c.f32(X_FREE, 256, p=1), c.f32(X_FREE, 256, p=1),
             c.f32(X_FREE + 256, 256, p=1), wait=[(c.sLD, ld)])
    r1 = p.v("reduce_sum", c.sm(S_MISC2, 1, p=1), c.f32(X_FREE, 256, p=1), AXX,
             wait=[(c.sDV, t1)])
    t2 = p.v("tensor_mul", c.f32(X_FREE + 512, 256, p=1), c.f32(X_FREE + 512, 256, p=1),
             c.f32(X_FREE + 768, 256, p=1))
    r2 = p.v("reduce_sum", c.sm(S_MISC2 + 1, 1, p=1), c.f32(X_FREE + 512, 256, p=1),
             AXX, wait=[(c.sDV, t2)])
    e1 = p.act(c.sm(S_MISC2 + 2, 1, p=1), c.sm(S_MISC2, 1, p=1), AF.Exp,
               wait=[(c.sDV, r2)], inc=c.sSC)
    e2 = p.act(c.sm(S_MISC2 + 3, 1, p=1), c.sm(S_MISC2 + 1, 1, p=1), AF.Exp, inc=c.sSC)
    sub = p.v("tensor_sub", c.sm(S_MISC2 + 2, 1, p=1), c.sm(S_MISC2 + 2, 1, p=1),
              c.sm(S_MISC2 + 3, 1, p=1), wait=[(c.sSC, e2)])
    lamv = p.v("tensor_scalar", c.sm(S_MISC2 + 2, 1, p=1), c.sm(S_MISC2 + 2, 1, p=1),
               0.2, -0.8, ALU.add, ALU.mult, wait=[(c.sDV, sub)])
    mmv = p.mm(c.p32[4][:, 0:1], c.sm(S_ONES1, 128, p=1), c.sm(S_MISC2 + 2, 1, p=1),
               start=True, stop=True, wait=[(c.sDV, lamv)], inc=c.sPE)
    p.v("tensor_copy", c.sm(S_LAM, 1), c.p32[4][:, 0:1], wait=[(c.sPE, mmv)])


def emit_l0a_rope(p, ps, tt, kind, mmv, trf, hg):
    c = p.c
    dstf = AT_QF if kind == "q" else AT_KF
    cp = p.v("tensor_copy", c.f32(X_X, 512), ps[:, 0:512], wait=[(c.sPE, mmv)])
    sq = p.act(c.bf(AT_SQ2, 512), c.f32(X_X, 512), AF.Square, wait=[(c.sDV, cp)],
               inc=c.sSC)
    rd = p.v("reduce_sum", c.sm(S_RST, 4),
             bass.AP(c.abf.ap().tensor, AT_SQ2, [[ABF_N, 128], [128, 4], [1, 128]]),
             AXX, wait=[(c.sSC, sq)])
    ts1 = p.v("tensor_scalar", c.sm(S_RST + 4, 4), c.sm(S_RST, 4), 1.0 / 128.0,
              1.1920929e-07, ALU.mult, ALU.add, wait=[(c.sDV, rd)])
    sq2 = p.act(c.sm(S_RST, 4), c.sm(S_RST + 4, 4), AF.Sqrt, wait=[(c.sDV, ts1)],
                inc=c.sSC)
    rc = p.v("reciprocal", c.sm(S_RST + 4, 4), c.sm(S_RST, 4), wait=[(c.sSC, sq2)])
    xin = bass.AP(c.af32.ap().tensor, X_X, [[AF32_N, 128], [128, 4], [1, 128]])
    xs = bass.AP(c.af32.ap().tensor, X_X + 512, [[AF32_N, 128], [128, 4], [1, 128]])
    rstd_b = bass.AP(c.asm.ap().tensor, S_RST + 4, [[ASM_N, 128], [1, 4], [0, 128]])
    sc = p.v("tensor_tensor", xs, xin, rstd_b, ALU.mult, wait=[(c.sDV, rc)])
    A = bass.AP(c.af32.ap().tensor, X_X + 512, [[AF32_N, 128], [256, 2], [128, 2], [1, 64]])
    Bv = bass.AP(c.af32.ap().tensor, X_X + 512 + 64, [[AF32_N, 128], [256, 2], [128, 2], [1, 64]])
    CC = bass.AP(c.af32.ap().tensor, X_BC + hg * 128, [[AF32_N, 128], [64, 2], [0, 2], [1, 64]])
    SS = bass.AP(c.af32.ap().tensor, X_BC + 512 + hg * 128, [[AF32_N, 128], [64, 2], [0, 2], [1, 64]])
    T1 = bass.AP(c.af32.ap().tensor, X_X + 1024, [[AF32_N, 128], [128, 2], [64, 2], [1, 64]])
    T2 = bass.AP(c.af32.ap().tensor, X_X + 1280, [[AF32_N, 128], [128, 2], [64, 2], [1, 64]])
    OA = bass.AP(c.abf.ap().tensor, AT_STG, [[ABF_N, 128], [256, 2], [128, 2], [1, 64]])
    OB = bass.AP(c.abf.ap().tensor, AT_STG + 64, [[ABF_N, 128], [256, 2], [128, 2], [1, 64]])
    v1 = p.v("tensor_tensor", T1, A, CC, ALU.mult,
             wait=[(c.sDV, sc), (c.sLD, c.l0a_pro)])
    v2 = p.v("tensor_tensor", T2, Bv, SS, ALU.mult)
    oa = p.v("tensor_tensor", OA, T1, T2, ALU.add, wait=[(c.sDV, v2)])
    v3 = p.v("tensor_tensor", T1, A, SS, ALU.mult, wait=[(c.sDV, oa)])
    v4 = p.v("tensor_tensor", T2, Bv, CC, ALU.mult)
    ob = p.v("tensor_tensor", OB, T2, T1, ALU.subtract, wait=[(c.sDV, v4)])
    emit_transpose_tm(p, AT_STG, dstf + tt * 128, 2048, 4, ob, trf)
    return cp


def emit_l0a(p, x_src, x_dst):
    c = p.c
    p.dma("sync", c.f32(X_BC, 512), c.hh["a_cos8"], wait=[W(c.sDV), W(c.sSC)])
    c.l0a_pro = p.dma("sync", c.f32(X_BC + 512, 512), c.hh["a_sin8"])
    emit_lam(p)
    st_free = {0: 0, 1: 0}
    pu_free = {0: 0, 1: 0}
    comb_done = {qs: 0 for qs in range(4)}
    for b in range(BPC):
        row0 = b * T
        trf = {}
        for cc in range(4):
            emit_ln_chunk(p, x_src, row0 + cc * 512, 4, fm_off=AT_XN, fm_stride=2048,
                          col0=cc * 512, tr_free=trf)
        xn_ready = c.sDV.v
        for hg in range(4):
            msv = p.v("memset", c.bf(AT_VA, 8320), 1.0, wait=[W(c.sPE)])
            evac_slot = {}
            for wname, kind in (("a_wqT", "q"), ("a_wkT", "k"), ("a_wvT", "v")):
                src = bass.AP(c.hh[wname].tensor, hg * 512,
                              [[D, 128], [128 * D, 16], [1, 512]])
                dst = bass.AP(c.abf.ap().tensor, AT_W,
                              [[ABF_N, 128], [512, 16], [1, 512]])
                wld = p.dma("gpsimd", dst, src, wait=[W(c.sPE)], sem=c.sG)
                for tt in range(16):
                    ps = c.p32[tt % 2]
                    w0 = [(c.sG, wld), (c.sDV, xn_ready)]
                    if (tt % 2) in evac_slot:
                        w0.append((c.sDV, evac_slot[tt % 2]))
                    mmv = None
                    for k in range(16):
                        mmv = p.mm(ps[:, 0:512], c.bf(AT_XN + k * 2048 + tt * 128, 128),
                                   c.bf(AT_W + k * 512, 512), start=(k == 0),
                                   stop=(k == 15), wait=w0 if k == 0 else (),
                                   inc=c.sPE if k == 15 else None)
                    if kind == "v":
                        dstv = bass.AP(c.abf.ap().tensor, AT_VA + tt * 520,
                                       [[ABF_N, 128], [257, 2], [1, 256]])
                        srcv = bass.AP(ps.ap().tensor, 0, [[512, 128], [256, 2], [1, 256]])
                        evac_slot[tt % 2] = p.v("tensor_copy", dstv, srcv,
                                                wait=[(c.sPE, mmv), (c.sDV, msv)])
                    else:
                        evac_slot[tt % 2] = emit_l0a_rope(p, ps, tt, kind, mmv, trf, hg)
            proj_done = c.sDV.v
            for h in range(2):
                head = hg * 2 + h
                for qc in range(4):
                    stage_ev = None
                    for i in (1, 2):
                        dt = h * 2 + (i - 1)
                        exp_done = {}
                        av_done = {}
                        for kt in range(qc * 4 + 4):
                            par = kt % 2
                            q0 = max(qc * 512, kt * 128)
                            N = qc * 512 + 512 - q0
                            psT = c.p32[4 + par]
                            w0 = [(c.sDV, proj_done), (c.sSC, st_free[par])]
                            mmv = p.mm(psT[:, 0:N], c.bf(AT_KF + dt * 2048 + kt * 128, 128),
                                       c.bf(AT_QF + dt * 2048 + q0, N), start=True,
                                       stop=True, wait=w0, inc=c.sPE)
                            ew = [(c.sPE, mmv), (c.sPE, pu_free[par])]
                            expv = p.act(c.bf(AT_PU + par * 512, N), psT[:, 0:N], AF.Exp,
                                         bias=c.sm(S_B12, 1), scale=c.sm(S_SCL, 1),
                                         wait=ew, inc=c.sSC)
                            exp_done[kt] = expv
                            st_free[par] = expv
                            dv_mask = None
                            if kt * 128 >= qc * 512:
                                dv_mask = p.v("tensor_mul", c.bf(AT_PU + par * 512, 128),
                                              c.bf(AT_PU + par * 512, 128),
                                              c.bf(A_CSLO, 128), wait=[(c.sSC, expv)])
                            avv = None
                            qss = [qs for qs in range(4) if qc * 4 + qs >= kt]
                            for qs in qss:
                                Q = qc * 4 + qs
                                col = Q * 128 - q0
                                w2 = [(c.sSC, expv)]
                                if dv_mask is not None:
                                    w2.append((c.sDV, dv_mask))
                                if kt == 0:
                                    if i == 2:
                                        w2.append((c.sDV, stage_ev))
                                    else:
                                        w2.append((c.sDV, comb_done[qs]))
                                avv = p.mm(c.p32[qs][:, 0:257],
                                           c.bf(AT_PU + par * 512 + col, 128),
                                           c.bf(AT_VA + kt * 520 + h * 257, 257),
                                           start=(kt == 0), stop=(kt == Q),
                                           wait=w2, inc=c.sPE if qs == qss[-1] else None)
                            av_done[kt] = avv
                            pu_free[par] = avv
                        if i == 1:
                            ev = None
                            for qs in range(4):
                                ev = p.v("tensor_copy", c.f32(X_X + qs * 264, 257),
                                         c.p32[qs][:, 0:257],
                                         wait=[(c.sPE, av_done[qc * 4 + qs])])
                            stage_ev = ev
                    trf2 = {}
                    for qs in range(4):
                        so = X_X + qs * 264
                        e1 = p.v("reciprocal", c.sm(S_MISC2 + 8, 1), c.f32(so + 256, 1),
                                 wait=[(c.sDV, stage_ev), (c.sPE, av_done[qc * 4 + qs])])
                        e2 = p.v("reciprocal", c.sm(S_MISC2 + 9, 1),
                                 c.p32[qs][:, 256:257], wait=[(c.sDV, e1)])
                        e3 = p.v("tensor_scalar_mul", c.sm(S_MISC2 + 8, 1),
                                 c.sm(S_MISC2 + 8, 1), 0.8, wait=[(c.sDV, e2)])
                        e4 = p.v("tensor_mul", c.sm(S_MISC2 + 9, 1), c.sm(S_MISC2 + 9, 1),
                                 c.sm(S_LAM, 1), wait=[(c.sDV, e3)])
                        e5 = p.v("tensor_scalar_mul", c.f32(so, 256), c.f32(so, 256),
                                 c.sm(S_MISC2 + 8, 1), wait=[(c.sDV, e4)])
                        e6 = p.v("scalar_tensor_tensor", c.bf(AT_PRE + (qs % 2) * 256, 256),
                                 c.p32[qs][:, 0:256], c.sm(S_MISC2 + 9, 1), c.f32(so, 256),
                                 ALU.mult, ALU.add, wait=[(c.sDV, e5)])
                        comb_done[qs] = e6
                        emit_transpose_tm(p, AT_PRE + (qs % 2) * 256, AT_STG + qs * 128,
                                          512, 2, e6, trf2,
                                          evac_wait=[(c.sST, getattr(c, "stg_gather", 0))])
                    srcg = bass.AP(c.abf.ap().tensor, AT_STG,
                                   [[ABF_N, 128], [512, 2], [128, 4], [1, 128]])
                    dstg = bass.AP(c.aout.tensor, (head * 256) * D + qc * 512,
                                   [[D, 128], [128 * D, 2], [128, 4], [1, 128]])
                    c.stg_gather = p.dma("sync", dstg, srcg, wait=[W(c.sDV)], sem=c.sST)
        emit_wo_proj(p, row0, x_src, x_dst, c.hh["a_woT"])


def build_l0a_test():
    es = ExitStack()
    with es:
        c = Ctx(es)
        x_in = c.nc.dram_tensor("x_in", [TOK, D], F32, kind="ExternalInput").ap()
        x_out = c.nc.dram_tensor("x_out", [TOK, D], F32, kind="ExternalOutput").ap()
        p = P(c)
        emit_consts(p)
        emit_l0a(p, x_in, x_out)
        emit_final_wait(p)
        p.flush()
    return c.nc



def emit_wo_proj(p, row0, x_src, x_dst, w_ap):
    c = p.c
    aout_done = c.sST.v
    wld = None
    for k in range(16):
        wld = p.dma("gpsimd", c.bf(AT_XN + k * 2048, 2048),
                    w_ap[k * 128:(k + 1) * 128, :],
                    wait=[W(c.sPE)] if k == 0 else (), sem=c.sG)
    wo_mm = {}
    wo_ev = {}
    for tt in range(16):
        par = tt % 2
        w0 = [(c.sST, aout_done)]
        if tt >= 2:
            w0.append((c.sPE, wo_mm[tt - 2]))
        srca = bass.AP(c.aout.tensor, tt * 128, [[D, 128], [128 * D, 16], [1, 128]])
        dsta = bass.AP(c.abf.ap().tensor, AT_W + par * 2048,
                       [[ABF_N, 128], [128, 16], [1, 128]])
        p.dma("sync", dsta, srca, wait=w0, sem=c.sXp[par])
        w1 = WST(c)
        if tt >= 2:
            w1.append((c.sDV, wo_ev[tt - 2]))
        xld = p.dma("sync", c.f32(X_X + par * 2048, 2048),
                    x_src[row0 + tt * 128: row0 + tt * 128 + 128, :], wait=w1,
                    sem=c.sXp[par])
        w2 = [(c.sXp[par], xld), (c.sG, wld)]
        if tt == 0:
            w2.append((c.sDV, c.sDV.v))
        mmv = None
        for dc in range(4):
            for k in range(16):
                mmv = p.mm(c.p32[dc][:, 0:512], c.bf(AT_W + par * 2048 + k * 128, 128),
                           c.bf(AT_XN + k * 2048 + dc * 512, 512),
                           start=(k == 0), stop=(k == 15),
                           wait=w2 if (dc == 0 and k == 0) else (),
                           inc=c.sPE if (dc == 3 and k == 15) else None)
        wo_mm[tt] = mmv
        ev = None
        for dc in range(4):
            ev = p.v("tensor_add", c.f32(X_X + par * 2048 + dc * 512, 512),
                     c.p32[dc][:, 0:512], c.f32(X_X + par * 2048 + dc * 512, 512),
                     wait=[(c.sPE, mmv)])
        wo_ev[tt] = ev
        p.dma("sync", x_dst[row0 + tt * 128: row0 + tt * 128 + 128, :],
              c.f32(X_X + par * 2048, 2048), wait=[(c.sDV, ev)], sem=c.sST)


def emit_l0b_rope(p, ps, tt, kind, mmv, trf):
    c = p.c
    dstf = AT_QF if kind == "q" else AT_KF
    cp = p.v("tensor_copy", c.f32(X_X, 512), ps[:, 0:512], wait=[(c.sPE, mmv)])
    X1 = bass.AP(c.af32.ap().tensor, X_X, [[AF32_N, 128], [128, 4], [1, 64]])
    X2 = bass.AP(c.af32.ap().tensor, X_X + 64, [[AF32_N, 128], [128, 4], [1, 64]])
    CC = bass.AP(c.asm.ap().tensor, S_ROPEC + tt * 64, [[ASM_N, 128], [0, 4], [1, 64]])
    SS = bass.AP(c.asm.ap().tensor, S_ROPES + tt * 64, [[ASM_N, 128], [0, 4], [1, 64]])
    T1 = bass.AP(c.af32.ap().tensor, X_X + 1024, [[AF32_N, 128], [64, 4], [1, 64]])
    T2 = bass.AP(c.af32.ap().tensor, X_X + 1280, [[AF32_N, 128], [64, 4], [1, 64]])
    OA = bass.AP(c.abf.ap().tensor, AT_STG, [[ABF_N, 128], [128, 4], [1, 64]])
    OB = bass.AP(c.abf.ap().tensor, AT_STG + 64, [[ABF_N, 128], [128, 4], [1, 64]])
    v1 = p.v("tensor_tensor", T1, X1, CC, ALU.mult, wait=[(c.sDV, cp)])
    v2 = p.v("tensor_tensor", T2, X2, SS, ALU.mult)
    oa = p.v("tensor_tensor", OA, T1, T2, ALU.subtract, wait=[(c.sDV, v2)])
    v3 = p.v("tensor_tensor", T1, X2, CC, ALU.mult, wait=[(c.sDV, oa)])
    v4 = p.v("tensor_tensor", T2, X1, SS, ALU.mult)
    ob = p.v("tensor_tensor", OB, T1, T2, ALU.add, wait=[(c.sDV, v4)])
    emit_transpose_tm(p, AT_STG, dstf + tt * 128, 2048, 4, ob, trf)
    return cp


def emit_l0b(p, x_src, x_dst):
    c = p.c
    st_free = {0: 0, 1: 0}
    pu_free = {0: 0, 1: 0}
    comb_done = {0: 0, 1: 0}
    for b in range(BPC):
        row0 = b * T
        trf = {}
        for cc in range(4):
            emit_ln_chunk(p, x_src, row0 + cc * 512, 4, fm_off=AT_XN, fm_stride=2048,
                          col0=cc * 512, tr_free=trf)
        xn_ready = c.sDV.v
        for hg in range(4):
            msv = p.v("memset", c.bf(AT_VA, 8320), 1.0, wait=[W(c.sPE)])
            evac_slot = {}
            for wname, kind in (("b_wq", "q"), ("b_wk", "k"), ("b_wv", "v")):
                src = bass.AP(c.hh[wname].tensor, hg * 512,
                              [[D, 128], [128 * D, 16], [1, 512]])
                dst = bass.AP(c.abf.ap().tensor, AT_W,
                              [[ABF_N, 128], [512, 16], [1, 512]])
                wld = p.dma("gpsimd", dst, src, wait=[W(c.sPE)], sem=c.sG)
                for tt in range(16):
                    ps = c.p32[tt % 2]
                    w0 = [(c.sG, wld), (c.sDV, xn_ready)]
                    if (tt % 2) in evac_slot:
                        w0.append((c.sDV, evac_slot[tt % 2]))
                    mmv = None
                    for k in range(16):
                        mmv = p.mm(ps[:, 0:512], c.bf(AT_XN + k * 2048 + tt * 128, 128),
                                   c.bf(AT_W + k * 512, 512), start=(k == 0),
                                   stop=(k == 15), wait=w0 if k == 0 else (),
                                   inc=c.sPE if k == 15 else None)
                    if kind == "v":
                        dstv = bass.AP(c.abf.ap().tensor, AT_VA + tt * 520,
                                       [[ABF_N, 128], [129, 4], [1, 128]])
                        srcv = bass.AP(ps.ap().tensor, 0, [[512, 128], [128, 4], [1, 128]])
                        evac_slot[tt % 2] = p.v("tensor_copy", dstv, srcv,
                                                wait=[(c.sPE, mmv), (c.sDV, msv)])
                    else:
                        evac_slot[tt % 2] = emit_l0b_rope(p, ps, tt, kind, mmv, trf)
            proj_done = c.sDV.v
            for h in range(4):
                head = hg * 4 + h
                dt = h
                for cw in range(8):
                    kts = list(range(max(0, 2 * cw - 2), 2 * cw + 2))
                    exp_done = {}
                    av_done = {}
                    first_kt = kts[0]
                    for kt in kts:
                        par = kt % 2
                        q0 = max(cw * 256, kt * 128)
                        N = cw * 256 + 256 - q0
                        psT = c.p32[4 + par]
                        w0 = [(c.sDV, proj_done), (c.sSC, st_free[par])]
                        mmv = p.mm(psT[:, 0:N], c.bf(AT_KF + dt * 2048 + kt * 128, 128),
                                   c.bf(AT_QF + dt * 2048 + q0, N), start=True,
                                   stop=True, wait=w0, inc=c.sPE)
                        ew = [(c.sPE, mmv), (c.sPE, pu_free[par])]
                        expv = p.act(c.bf(AT_PU + par * 512, N), psT[:, 0:N], AF.Exp,
                                     bias=c.sm(S_B24, 1), scale=c.sm(S_SCL, 1),
                                     wait=ew, inc=c.sSC)
                        exp_done[kt] = expv
                        st_free[par] = expv
                        dv_mask = None
                        if kt * 128 >= cw * 256:
                            dv_mask = p.v("tensor_mul", c.bf(AT_PU + par * 512, 128),
                                          c.bf(AT_PU + par * 512, 128),
                                          c.bf(A_CSLO, 128), wait=[(c.sSC, expv)])
                        qss = [qs for qs in range(2) if 2 * cw + qs >= kt]
                        avv = None
                        for qs in qss:
                            Q = 2 * cw + qs
                            col = Q * 128 - q0
                            w2 = [(c.sSC, expv)]
                            if dv_mask is not None:
                                w2.append((c.sDV, dv_mask))
                            if kt == first_kt:
                                w2.append((c.sDV, comb_done[qs]))
                            avv = p.mm(c.p32[qs][:, 0:129],
                                       c.bf(AT_PU + par * 512 + col, 128),
                                       c.bf(AT_VA + kt * 520 + h * 129, 129),
                                       start=(kt == first_kt), stop=(kt == Q),
                                       wait=w2, inc=c.sPE if qs == qss[-1] else None)
                        av_done[kt] = avv
                        pu_free[par] = avv
                    trf2 = {}
                    for qs in range(2):
                        e1 = p.v("reciprocal", c.sm(S_MISC2 + 8, 1),
                                 c.p32[qs][:, 128:129],
                                 wait=[(c.sPE, av_done[2 * cw + qs])])
                        e2 = p.v("tensor_scalar_mul", c.bf(AT_PRE + qs * 256, 128),
                                 c.p32[qs][:, 0:128], c.sm(S_MISC2 + 8, 1),
                                 wait=[(c.sDV, e1)])
                        comb_done[qs] = e2
                        emit_transpose_tm(p, AT_PRE + qs * 256, AT_STG + qs * 128,
                                          999999, 1, e2, trf2,
                                          evac_wait=[(c.sST, getattr(c, "stg_gather", 0))])
                    srcg = bass.AP(c.abf.ap().tensor, AT_STG,
                                   [[ABF_N, 128], [128, 2], [1, 128]])
                    dstg = bass.AP(c.aout.tensor, (head * 128) * D + cw * 256,
                                   [[D, 128], [128, 2], [1, 128]])
                    c.stg_gather = p.dma("sync", dstg, srcg, wait=[W(c.sDV)], sem=c.sST)
        emit_wo_proj(p, row0, x_src, x_dst, c.hh["b_wo"])


def build_l0b_test():
    es = ExitStack()
    with es:
        c = Ctx(es)
        x_in = c.nc.dram_tensor("x_in", [TOK, D], F32, kind="ExternalInput").ap()
        x_out = c.nc.dram_tensor("x_out", [TOK, D], F32, kind="ExternalOutput").ap()
        p = P(c)
        emit_consts(p)
        emit_l0b(p, x_in, x_out)
        emit_final_wait(p)
        p.flush()
    return c.nc



LB_KTM = 2048
LB_BM = 6144
LB_GAT = 6240
LB_QT = 6272
LB_SEL = 6464
LB_STAGE = 6528


def emit_l1b_rope(p, ps, tt, kind, mmv, trf, width):
    c = p.c
    nh = width // 128
    dstf = AT_QF if kind == "q" else AT_KF
    cp = p.v("tensor_copy", c.f32(X_X, width), ps[:, 0:width], wait=[(c.sPE, mmv)])
    E1 = bass.AP(c.af32.ap().tensor, X_X, [[AF32_N, 128], [128, nh], [2, 64]])
    E2 = bass.AP(c.af32.ap().tensor, X_X + 1, [[AF32_N, 128], [128, nh], [2, 64]])
    CC = bass.AP(c.asm.ap().tensor, S_ROPEC + tt * 64, [[ASM_N, 128], [0, nh], [1, 64]])
    SS = bass.AP(c.asm.ap().tensor, S_ROPES + tt * 64, [[ASM_N, 128], [0, nh], [1, 64]])
    T1 = bass.AP(c.af32.ap().tensor, X_X + 1024, [[AF32_N, 128], [64, nh], [1, 64]])
    T2 = bass.AP(c.af32.ap().tensor, X_X + 1024 + 64 * nh, [[AF32_N, 128], [64, nh], [1, 64]])
    roff = (X_X + 512) if kind == "q" else (LB_KTM + tt * 256)
    OA = bass.AP(c.af32.ap().tensor, roff, [[AF32_N, 128], [128, nh], [2, 64]])
    OB = bass.AP(c.af32.ap().tensor, roff + 1, [[AF32_N, 128], [128, nh], [2, 64]])
    v1 = p.v("tensor_tensor", T1, E1, CC, ALU.mult, wait=[(c.sDV, cp)])
    v2 = p.v("tensor_tensor", T2, E2, SS, ALU.mult)
    oa = p.v("tensor_tensor", OA, T1, T2, ALU.subtract, wait=[(c.sDV, v2)])
    v3 = p.v("tensor_tensor", T1, E1, SS, ALU.mult, wait=[(c.sDV, oa)])
    v4 = p.v("tensor_tensor", T2, E2, CC, ALU.mult)
    ob = p.v("tensor_tensor", OB, T1, T2, ALU.add, wait=[(c.sDV, v4)])
    cb = p.v("tensor_copy", c.bf(AT_STG, width), c.f32(roff, width),
             wait=[(c.sDV, ob)])
    emit_transpose_tm(p, AT_STG, dstf + tt * 128, 2048, nh, cb, trf)
    return cp


def emit_l1b(p, x_src, x_dst):
    c = p.c
    st_free = {0: 0, 1: 0}
    pu_free = {0: 0, 1: 0}
    comb_done = {0: 0, 1: 0, 2: 0, 3: 0}
    for b in range(BPC):
        row0 = b * T
        trf = {}
        for cc in range(4):
            emit_ln_chunk(p, x_src, row0 + cc * 512, 4, fm_off=AT_XN, fm_stride=2048,
                          col0=cc * 512, tr_free=trf)
        xn_ready = c.sDV.v
        for hg in range(4):
            msv = p.v("memset", c.bf(AT_VA, 4160), 1.0, wait=[W(c.sPE)])
            evac_slot = {}
            src = bass.AP(c.hh["c_wk"].tensor, hg * 256,
                          [[1024, 128], [128 * 1024, 16], [1, 256]])
            dst = bass.AP(c.abf.ap().tensor, AT_W, [[ABF_N, 128], [256, 16], [1, 256]])
            wld = p.dma("gpsimd", dst, src, wait=[W(c.sPE)], sem=c.sG)
            for tt in range(16):
                ps = c.p32[tt % 2]
                w0 = [(c.sG, wld), (c.sDV, xn_ready)]
                if (tt % 2) in evac_slot:
                    w0.append((c.sDV, evac_slot[tt % 2]))
                mmv = None
                for k in range(16):
                    mmv = p.mm(ps[:, 0:256], c.bf(AT_XN + k * 2048 + tt * 128, 128),
                               c.bf(AT_W + k * 256, 256), start=(k == 0),
                               stop=(k == 15), wait=w0 if k == 0 else (),
                               inc=c.sPE if k == 15 else None)
                evac_slot[tt % 2] = emit_l1b_rope(p, ps, tt, "k", mmv, trf, 256)
            k_done = c.sDV.v
            bm_mm = {}
            for kvh in range(2):
                for kt in range(16):
                    bm_mm[kvh] = p.mm(c.p32[4 + kvh][:, 0:16],
                                      c.f32(LB_KTM + kt * 256 + kvh * 128, 128),
                                      c.sm(S_IND + kt * 16, 16),
                                      start=(kt == 0), stop=(kt == 15),
                                      wait=[(c.sDV, k_done), (c.sSC, st_free[kvh])] if kt == 0 else (),
                                      inc=c.sPE if kt == 15 else None)
            bme = None
            for kvh in range(2):
                bme = p.v("tensor_copy", c.f32(LB_BM + kvh * 16, 16),
                          c.p32[4 + kvh][:, 0:16], wait=[(c.sPE, bm_mm[kvh])])
            src = bass.AP(c.hh["c_wq"].tensor, hg * 512,
                          [[D, 128], [128 * D, 16], [1, 512]])
            dst = bass.AP(c.abf.ap().tensor, AT_W, [[ABF_N, 128], [512, 16], [1, 512]])
            wld = p.dma("gpsimd", dst, src, wait=[(c.sPE, c.sPE.v)], sem=c.sG)
            for tt in range(16):
                ps = c.p32[tt % 2]
                w0 = [(c.sG, wld), (c.sDV, xn_ready)]
                if (tt % 2) in evac_slot:
                    w0.append((c.sDV, evac_slot[tt % 2]))
                mmv = None
                for k in range(16):
                    mmv = p.mm(ps[:, 0:512], c.bf(AT_XN + k * 2048 + tt * 128, 128),
                               c.bf(AT_W + k * 512, 512), start=(k == 0),
                               stop=(k == 15), wait=w0 if k == 0 else (),
                               inc=c.sPE if k == 15 else None)
                evac_slot[tt % 2] = emit_l1b_rope(p, ps, tt, "q", mmv, trf, 512)
                if tt >= 8:
                    j = tt
                    for h in range(4):
                        trv = p.trans(c.p32[2][:, 0:128],
                                      c.f32(X_X + 512 + h * 128, 128),
                                      c.sm(S_IDENT, 128),
                                      wait=[(c.sDV, evac_slot[tt % 2]), (c.sDV, c.sDV.v)],
                                      inc=c.sPE)
                        qte = p.v("tensor_copy", c.f32(LB_QT, 128), c.p32[2][:, 0:128],
                                  wait=[(c.sPE, trv)])
                        gmm = p.mm(c.p32[3][:, 0:16], c.f32(LB_QT, 128),
                                   c.f32(LB_BM + (h // 2) * 16, 16),
                                   start=True, stop=True,
                                   wait=[(c.sDV, qte), (c.sDV, bme)], inc=c.sPE)
                        gcp = p.v("tensor_copy", c.f32(LB_GAT, 16), c.p32[3][:, 0:16],
                                  wait=[(c.sPE, gmm)])
                        gb = bass.AP(c.af32.ap().tensor, LB_GAT + j,
                                     [[AF32_N, 128], [0, j]])
                        ge = p.v("tensor_tensor", c.f32(LB_GAT + 16, j),
                                 c.f32(LB_GAT, j), gb, ALU.is_ge,
                                 wait=[(c.sDV, gcp)])
                        cnt = p.v("reduce_sum", c.f32(LB_GAT + 32, 1),
                                  c.f32(LB_GAT + 16, j), AXX, wait=[(c.sDV, ge)])
                        p.v("tensor_single_scalar",
                            c.f32(LB_SEL + h * 8 + (j - 8), 1),
                            c.f32(LB_GAT + 32, 1), 8.0, ALU.is_lt,
                            wait=[(c.sDV, cnt)])
            src = bass.AP(c.hh["c_wv"].tensor, hg * 256,
                          [[1024, 128], [128 * 1024, 16], [1, 256]])
            dst = bass.AP(c.abf.ap().tensor, AT_W, [[ABF_N, 128], [256, 16], [1, 256]])
            wld = p.dma("gpsimd", dst, src, wait=[(c.sPE, c.sPE.v)], sem=c.sG)
            for tt in range(16):
                ps = c.p32[tt % 2]
                w0 = [(c.sG, wld), (c.sDV, xn_ready)]
                if (tt % 2) in evac_slot:
                    w0.append((c.sDV, evac_slot[tt % 2]))
                mmv = None
                for k in range(16):
                    mmv = p.mm(ps[:, 0:256], c.bf(AT_XN + k * 2048 + tt * 128, 128),
                               c.bf(AT_W + k * 256, 256), start=(k == 0),
                               stop=(k == 15), wait=w0 if k == 0 else (),
                               inc=c.sPE if k == 15 else None)
                dstv = bass.AP(c.abf.ap().tensor, AT_VA + tt * 260,
                               [[ABF_N, 128], [129, 2], [1, 128]])
                srcv = bass.AP(ps.ap().tensor, 0, [[512, 128], [128, 2], [1, 128]])
                evac_slot[tt % 2] = p.v("tensor_copy", dstv, srcv,
                                        wait=[(c.sPE, mmv), (c.sDV, msv)])
            proj_done = c.sDV.v
            for h in range(4):
                head = hg * 4 + h
                dt = h
                kdt = h // 2
                for qc in range(8):
                    qts = (2 * qc, 2 * qc + 1)
                    kts = list(range(8)) if qc <= 3 else list(range(2 * qc + 2))
                    av_done = {}
                    for kt in kts:
                        par = kt % 2
                        if kt <= 7:
                            q0 = qc * 256
                            N = 256
                        else:
                            q0 = max(qc * 256, kt * 128)
                            N = qc * 256 + 256 - q0
                        psT = c.p32[4 + par]
                        w0 = [(c.sDV, proj_done), (c.sSC, st_free[par])]
                        mmv = p.mm(psT[:, 0:N], c.bf(AT_KF + kdt * 2048 + kt * 128, 128),
                                   c.bf(AT_QF + dt * 2048 + q0, N), start=True,
                                   stop=True, wait=w0, inc=c.sPE)
                        ew = [(c.sPE, mmv), (c.sPE, pu_free[par])]
                        expv = p.act(c.bf(AT_PU + par * 512, N), psT[:, 0:N], AF.Exp,
                                     bias=c.sm(S_B24, 1), scale=c.sm(S_SCL, 1),
                                     wait=ew, inc=c.sSC)
                        st_free[par] = expv
                        dv_mask = None
                        anti = kt >= 8 and kt * 128 >= qc * 256
                        if anti:
                            avm = p.v("tensor_mul", c.bf(AT_SQ2, 128),
                                      c.bf(AT_PU + par * 512, 128),
                                      c.bf(A_CSHI, 128),
                                      wait=[(c.sSC, expv),
                                            (c.sPE, getattr(c, "anti_free", 0))])
                            dv_mask = p.v("tensor_mul", c.bf(AT_PU + par * 512, 128),
                                          c.bf(AT_PU + par * 512, 128),
                                          c.bf(A_CSLO, 128), wait=[(c.sDV, avm)])
                        qss = [qs for qs in range(2) if kt <= 7 or qts[qs] >= kt]
                        avv = None
                        for qs in qss:
                            Q = qts[qs]
                            col = Q * 128 - q0
                            w2 = [(c.sSC, expv)]
                            if dv_mask is not None:
                                w2.append((c.sDV, dv_mask))
                            if kt == 0:
                                w2.append((c.sDV, comb_done[qs]))
                            stop = (kt == 7) if qc <= 3 else (kt == Q)
                            avv = p.mm(c.p32[qs][:, 0:129],
                                       c.bf(AT_PU + par * 512 + col, 128),
                                       c.bf(AT_VA + kt * 260 + kdt * 129, 129),
                                       start=(kt == 0), stop=stop,
                                       wait=w2, inc=c.sPE if qs == qss[-1] else None)
                            if anti and Q == kt:
                                avv = p.mm(c.p32[2 + qs][:, 0:129],
                                           c.bf(AT_SQ2, 128),
                                           c.bf(AT_VA + kt * 260 + kdt * 129, 129),
                                           start=True, stop=True,
                                           wait=[(c.sDV, avm),
                                                 (c.sDV, comb_done[qs])], inc=c.sPE)
                                c.anti_free = avv
                        av_done[kt] = avv
                        pu_free[par] = avv
                    av_all = c.sPE.v
                    trf2 = {}
                    for qs in range(2):
                        Q = qts[qs]
                        pm = c.p32[qs]
                        wv = [(c.sPE, av_all)]
                        if Q <= 7:
                            e1 = p.v("reciprocal", c.sm(S_MISC2 + 8, 1),
                                     pm[:, 128:129], wait=wv)
                            e2 = p.v("tensor_scalar_mul", c.bf(AT_PRE + qs * 256, 128),
                                     pm[:, 0:128], c.sm(S_MISC2 + 8, 1),
                                     wait=[(c.sDV, e1)])
                        else:
                            pa = c.p32[2 + qs]
                            cp1 = p.v("tensor_copy", c.f32(LB_STAGE, 129),
                                      pm[:, 0:129], wait=wv)
                            sel = c.f32(LB_SEL + h * 8 + (Q - 8), 1)
                            st1 = p.v("scalar_tensor_tensor", c.f32(LB_STAGE, 129),
                                      pa[:, 0:129], sel, c.f32(LB_STAGE, 129),
                                      ALU.mult, ALU.add, wait=[(c.sDV, cp1)])
                            e1 = p.v("reciprocal", c.sm(S_MISC2 + 8, 1),
                                     c.f32(LB_STAGE + 128, 1), wait=[(c.sDV, st1)])
                            e2 = p.v("tensor_scalar_mul", c.bf(AT_PRE + qs * 256, 128),
                                     c.f32(LB_STAGE, 128), c.sm(S_MISC2 + 8, 1),
                                     wait=[(c.sDV, e1)])
                        comb_done[qs] = e2
                        emit_transpose_tm(p, AT_PRE + qs * 256, AT_STG + qs * 128,
                                          999999, 1, e2, trf2,
                                          evac_wait=[(c.sST, getattr(c, "stg_gather", 0))])
                    srcg = bass.AP(c.abf.ap().tensor, AT_STG,
                                   [[ABF_N, 128], [128, 2], [1, 128]])
                    dstg = bass.AP(c.aout.tensor, (head * 128) * D + qc * 256,
                                   [[D, 128], [128, 2], [1, 128]])
                    c.stg_gather = p.dma("sync", dstg, srcg, wait=[W(c.sDV)], sem=c.sST)
        emit_wo_proj(p, row0, x_src, x_dst, c.hh["c_wo"])


def build_l1b_test():
    es = ExitStack()
    with es:
        c = Ctx(es)
        x_in = c.nc.dram_tensor("x_in", [TOK, D], F32, kind="ExternalInput").ap()
        x_out = c.nc.dram_tensor("x_out", [TOK, D], F32, kind="ExternalOutput").ap()
        p = P(c)
        emit_consts(p)
        emit_l1b(p, x_in, x_out)
        emit_final_wait(p)
        p.flush()
    return c.nc



def emit_l1a(p, xfull, delta_dst, s0, xstride=None):
    c = p.c
    p.dma("sync", c.af32[0:1, X_BC:X_BC + 2048], c.hh["h_lng"],
          wait=[W(c.sDV), W(c.sSC)] + WST(c))
    bld = p.dma("sync", c.af32[0:1, 0:2048], c.hh["h_lnb"])
    emit_bcast_row(p, c.af32[0:1, X_BC:X_BC + 2048], c.f32(X_BC, 2048), 2048,
                   wait=[(c.sLD, bld)])
    emit_bcast_row(p, c.af32[0:1, 0:2048],
                   bass.AP(c.abf.ap().tensor, A_GBETA, [[ABF_N, 128], [1, 2048]]),
                   2048, wait=[(c.sLD, bld)])
    bin_ld = p.dma("sync", c.sm(S_GSC, 48), c.hh["h_bin"])
    win_mm = {}
    upev = {}
    upst = {}
    if xstride is None:
        xstride = T
    for l in range(16):
        emit_ln_chunk(p, xfull, l * xstride + s0, 2, inner=True)
        xn_ready = c.sDV.v
        for cht in range(48):
            n = l * 48 + cht
            bs = n % 2
            wwait = []
            if n >= 2:
                wwait.append((c.sPE, win_mm[n - 2]))
            src = bass.AP(c.hh["h_winT"].tensor, cht * 128,
                          [[CH3, 128], [128 * CH3, 16], [1, 128]])
            dst = bass.AP(c.abf.ap().tensor, A_WG + bs * 2048,
                          [[ABF_N, 128], [128, 16], [1, 128]])
            wld = p.dma("gpsimd", dst, src, wait=wwait, sem=c.sGp[bs])
            ps = c.p32[bs]
            w0 = [(c.sGp[bs], wld), (c.sDV, xn_ready), (c.sLD, bin_ld)]
            if n >= 2:
                w0.append((c.sDV, upev[n - 2]))
            mmv = None
            for k in range(16):
                mmv = p.mm(ps[:, 0:256], c.bf(A_WG + bs * 2048 + k * 128, 128),
                           c.bf(A_FM + k * 256, 256), start=(k == 0), stop=(k == 15),
                           wait=w0 if k == 0 else (), inc=c.sPE if k == 15 else None)
            win_mm[n] = mmv
            ew = [(c.sPE, mmv)]
            if n >= 2:
                ew.append((c.sS[bs], upst[n - 2]))
            ev = p.v("tensor_scalar_add", c.bf(A_SILU + bs * 512, 256), ps[:, 0:256],
                     c.sm(S_GSC + cht, 1), wait=ew)
            upev[n] = ev
            upst[n] = p.dma("sync", c.upb[cht * 128:(cht + 1) * 128, l * 256:(l + 1) * 256],
                            c.bf(A_SILU + bs * 512, 256), wait=[(c.sDV, ev)], sem=c.sS[bs])
    upb_done = [W(c.sSa), W(c.sSb)]
    sbd_ld = p.dma("sync", c.f32(0, 768), c.hh["h_sbdup"], wait=[W(c.sDV)])
    cbd_ld = p.dma("sync", c.f32(768, 256), c.hh["h_cbdup"])
    CV_UP = A_H
    CV_MS = A_H + 2048
    CV_SC = A_H + 4096
    CV_VG = A_H + 5120
    CV_Y1 = A_H + 5632
    lmm = {}
    lev = {}
    yst = {}
    for t in range(256):
        bs = t % 2
        wwu = list(upb_done)
        if t >= 2:
            wwu.append((c.sPE, lmm[t - 2]))
        ldu = None
        for ci, grp in enumerate((t, 256 + t, 512 + t)):
            src = bass.AP(c.upb.tensor, (grp * 8) * 4096,
                          [[4096, 8], [256, 16], [1, 256]])
            ldu = p.dma("sync", c.bf(CV_UP + (bs * 3 + ci) * 256, 256), src,
                        wait=wwu if ci == 0 else (), sem=c.sW[bs])
            ms = bass.AP(c.hh["h_mshort"].tensor, grp * 128 * 128,
                         [[128, 128], [1, 128]])
            p.dma("gpsimd", c.bf(CV_MS + bs * 1024 + ci * 128, 128), ms,
                  wait=wwu if ci == 0 else (), sem=c.sGp[bs])
        ml = bass.AP(c.hh["h_mlong"].tensor, t * 128 * 128, [[128, 128], [1, 128]])
        mldv = p.dma("gpsimd", c.bf(CV_MS + bs * 1024 + 384, 128), ml, sem=c.sGp[bs])
        scev = None
        for ci in range(3):
            grp = (t, 256 + t, 512 + t)[ci]
            ps = c.p32[ci]
            w0 = [(c.sW[bs], ldu), (c.sGp[bs], mldv), (c.sLD, cbd_ld)]
            if t >= 2:
                w0.append((c.sDV, lev[t - 2]))
            mmv = p.mm(ps[:, 0:256], c.bf(CV_MS + bs * 1024 + ci * 128, 128),
                       c.bf(CV_UP + (bs * 3 + ci) * 256, 256), start=True, stop=True,
                       wait=w0, inc=c.sPE)
            scev = p.v("tensor_scalar_add", c.bf(CV_SC + ci * 256, 256), ps[:, 0:256],
                       c.f32(grp, 1), wait=[(c.sPE, mmv)])
        w1 = [(c.sDV, scev)]
        if t >= 2:
            w1.append((c.sDV, lev[t - 2]))
        vg = p.v("tensor_mul", c.bf(CV_VG + bs * 256, 256), c.bf(CV_SC + 2 * 256, 256),
                 c.bf(CV_SC + 1 * 256, 256), wait=w1)
        lmm[t] = p.mm(c.p32[3][:, 0:256], c.bf(CV_MS + bs * 1024 + 384, 128),
                      c.bf(CV_VG + bs * 256, 256), start=True, stop=True,
                      wait=[(c.sDV, vg)], inc=c.sPE)
        yw = [(c.sPE, lmm[t]), (c.sLD, cbd_ld)]
        if t >= 2:
            yw.append((c.sS[bs], yst[t - 2]))
        y1 = p.v("scalar_tensor_tensor", c.bf(CV_Y1 + bs * 256, 256),
                 c.bf(CV_VG + bs * 256, 256), c.f32(768 + t, 1),
                 c.p32[3][:, 0:256], ALU.mult, ALU.add, wait=yw)
        y2 = p.v("tensor_mul", c.bf(CV_Y1 + bs * 256, 256), c.bf(CV_Y1 + bs * 256, 256),
                 c.bf(CV_SC, 256), wait=[(c.sDV, y1)])
        lev[t] = y2
        yst[t] = p.dma("sync",
                       bass.AP(c.yb.tensor, (t * 8) * 4096, [[4096, 8], [256, 16], [1, 256]]),
                       c.bf(CV_Y1 + bs * 256, 256), wait=[(c.sDV, y2)], sem=c.sS[bs])
    yb_done = [W(c.sSa), W(c.sSb)]
    wld = None
    for k in range(16):
        wld = p.dma("gpsimd", c.bf(AT_XN + k * 2048, 2048),
                    c.hh["h_woutT"][k * 128:(k + 1) * 128, :],
                    wait=[W(c.sPE)] if k == 0 else (), sem=c.sG)
    wo_mm = {}
    wo_ev = {}
    dst_st = {}
    for tt in range(32):
        l, si = tt // 2, tt % 2
        par = tt % 2
        w0 = list(yb_done)
        if tt >= 2:
            w0.append((c.sPE, wo_mm[tt - 2]))
        srca = bass.AP(c.yb.tensor, l * 256 + si * 128,
                       [[4096, 128], [128 * 4096, 16], [1, 128]])
        dsta = bass.AP(c.abf.ap().tensor, AT_W + par * 2048,
                       [[ABF_N, 128], [128, 16], [1, 128]])
        ald = p.dma("sync", dsta, srca, wait=w0, sem=c.sXp[par])
        w2 = [(c.sXp[par], ald), (c.sG, wld)]
        if tt == 0:
            w2.append((c.sDV, c.sDV.v))
        mmv = None
        for dc in range(4):
            for k in range(16):
                mmv = p.mm(c.p32[dc][:, 0:512], c.bf(AT_W + par * 2048 + k * 128, 128),
                           c.bf(AT_XN + k * 2048 + dc * 512, 512),
                           start=(k == 0), stop=(k == 15),
                           wait=w2 if (dc == 0 and k == 0) else (),
                           inc=c.sPE if (dc == 3 and k == 15) else None)
        wo_mm[tt] = mmv
        ew = [(c.sPE, mmv)]
        if tt >= 2:
            ew.append((c.sS[par], dst_st[tt - 2]))
        ev = None
        for dc in range(4):
            ev = p.v("tensor_copy", c.f32(X_X + par * 2048 + dc * 512, 512),
                     c.p32[dc][:, 0:512], wait=ew if dc == 0 else [(c.sPE, mmv)])
        wo_ev[tt] = ev
        r0 = l * xstride + s0 + si * 128
        dst_st[tt] = p.dma("sync", delta_dst[r0:r0 + 128, :],
                           c.f32(X_X + par * 2048, 2048),
                           wait=[(c.sDV, ev)], sem=c.sS[par])


def build_l1a_test(core, dbg=False):
    es = ExitStack()
    with es:
        c = Ctx(es)
        xfull = c.nc.dram_tensor("xfull", [B * T, D], F32, kind="ExternalInput").ap()
        dout = c.nc.dram_tensor("delta", [B * T, D], F32, kind="ExternalOutput").ap()
        if dbg:
            dup = c.nc.dram_tensor("dbg_up", [CH3, 4096], BF16, kind="ExternalOutput").ap()
            dyb = c.nc.dram_tensor("dbg_yb", [D, 4096], BF16, kind="ExternalOutput").ap()
        p = P(c)
        emit_consts(p)
        emit_l1a(p, xfull, dout, core * 256)
        if dbg:
            for r in range(0, CH3, 1024):
                p.dma("sync", dup[r:r+1024, :].rearrange("(a p) n -> p (a n)", p=128),
                      c.nc, wait=[W(c.sST)], sem=c.sST) if False else None
            p.dma("gpsimd", dup, c.upb, wait=[W(c.sST)], sem=c.sST)
            p.dma("gpsimd", dyb, c.yb, sem=c.sST)
        emit_final_wait(p)
        p.flush()
    return c.nc



def build_ffn_test(L=0, ext_w=False):
    es = ExitStack()
    with es:
        c = Ctx(es)
        x_in = c.nc.dram_tensor("x_in", [TOK, D], F32, kind="ExternalInput").ap()
        x_out = c.nc.dram_tensor("x_out", [TOK, D], F32, kind="ExternalOutput").ap()
        p = P(c)
        emit_consts(p)
        if ext_w:
            c.wscr[f"g{L}"] = c.nc.dram_tensor("wgb", [D, I_FFN], BF16, kind="ExternalInput").ap()
            c.wscr[f"u{L}"] = c.nc.dram_tensor("wub", [D, I_FFN], BF16, kind="ExternalInput").ap()
            c.wscr[f"d{L}"] = c.nc.dram_tensor("wdb", [I_FFN, D], BF16, kind="ExternalInput").ap()
        else:
            emit_ffn_dequant(p, L)
        dbg = {"xn0": c.nc.dram_tensor("dbg_xn0", [128, 8192], BF16, kind="ExternalOutput").ap(),
               "h0": c.nc.dram_tensor("dbg_h0", [128, 32768], BF16, kind="ExternalOutput").ap(),
               "st0": c.nc.dram_tensor("dbg_st0", [128, 512], F32, kind="ExternalOutput").ap(),
               "xt0": c.nc.dram_tensor("dbg_xt0", [128, 8192], F32, kind="ExternalOutput").ap()}
        emit_ffn_call(p, L, x_in, x_out, dbg=dbg)
        emit_final_wait(p)
        p.flush()
    return c.nc





NEEDS = {
    "l0a": {"a_wqT", "a_wkT", "a_wvT", "a_woT", "a_lq1", "a_lk1", "a_lq2",
            "a_lk2", "a_cos8", "a_sin8"},
    "ffn0": {"f0_g8", "f0_u8", "f0_d8", "f0_gsc", "f0_usc", "f0_dsc"},
    "ffn1": {"f1_g8", "f1_u8", "f1_d8", "f1_gsc", "f1_usc", "f1_dsc"},
    "l0b": {"b_wq", "b_wk", "b_wv", "b_wo"},
    "l1a": {"h_lng", "h_lnb", "h_winT", "h_bin", "h_mshort", "h_sbdup",
            "h_mlong", "h_cbdup", "h_woutT"},
    "l1b": {"c_wq", "c_wk", "c_wv", "c_wo"},
}


def _build_stage(stage, L=0, delta=False):
    es = ExitStack()
    with es:
        c = Ctx(es, need=NEEDS[stage if not stage.startswith("ffn") else f"ffn{L}"])
        x_in = c.nc.dram_tensor("x_in", [TOK, D], F32, kind="ExternalInput").ap()
        x_out = c.nc.dram_tensor("x_out", [TOK, D], F32, kind="ExternalOutput").ap()
        d_in = None
        if delta:
            d_in = c.nc.dram_tensor("d_in", [TOK, D], F32, kind="ExternalInput").ap()
        p = P(c)
        emit_consts(p)
        if stage == "l0a":
            emit_l0a(p, x_in, x_out)
        elif stage == "ffn":
            emit_ffn_dequant(p, L)
            emit_ffn_call(p, L, x_in, x_out, delta_src=d_in)
        elif stage == "l0b":
            emit_l0b(p, x_in, x_out)
        elif stage == "l1b":
            emit_l1b(p, x_in, x_out)
        emit_final_wait(p)
        p.flush()
    return c.nc


def _build_l1a_stage():
    es = ExitStack()
    with es:
        c = Ctx(es, need=NEEDS["l1a"])
        xfull = c.nc.dram_tensor("xfull", [TOK, D], F32, kind="ExternalInput").ap()
        dout = c.nc.dram_tensor("delta", [TOK, D], F32, kind="ExternalOutput").ap()
        p = P(c)
        emit_consts(p)
        emit_l1a(p, xfull, dout, 0, xstride=256)
        emit_final_wait(p)
        p.flush()
    return c.nc


_PROGS = {}


def _prog(key):
    if key not in _PROGS:
        if key == "l1a":
            _PROGS[key] = _build_l1a_stage()
        elif key == "ffn1d":
            _PROGS[key] = _build_stage("ffn", L=1, delta=True)
        elif key.startswith("ffn"):
            _PROGS[key] = _build_stage("ffn", L=int(key[3]))
        else:
            _PROGS[key] = _build_stage(key)
    return _PROGS[key]


def _launch(nc, host, per_core, needs, extra=()):
    in_maps = []
    for cidx in range(NCORES):
        m = {n: host[n] for n in needs | CONST_NAMES}
        for k, arrs in per_core.items():
            m[k] = arrs[cidx]
        in_maps.append(m)
    return run_bass_kernel_spmd(nc, in_maps, core_ids=list(range(NCORES)))


def kernel(**inputs):
    host = prep_host(inputs)
    x = np.ascontiguousarray(np.asarray(inputs["L0_attn_a_x"], dtype=np.float32)
                             .reshape(B * T, D))
    shards = [np.ascontiguousarray(x[cc * TOK:(cc + 1) * TOK]) for cc in range(NCORES)]

    def run(key, needskey, per_core):
        res = _launch(_prog(key), host, per_core, NEEDS[needskey])
        return [np.asarray(res.results[cc]["x_out" if key != "l1a" else "delta"])
                for cc in range(NCORES)]

    shards = run("l0a", "l0a", {"x_in": shards})
    shards = run("ffn0", "ffn0", {"x_in": shards})
    shards = run("l0b", "l0b", {"x_in": shards})
    shards = run("ffn0", "ffn0", {"x_in": shards})
    xf = np.concatenate(shards, axis=0).reshape(B, T, D)
    xs = [np.ascontiguousarray(xf[:, cc * 256:(cc + 1) * 256, :].reshape(TOK, D))
          for cc in range(NCORES)]
    dslices = run("l1a", "l1a", {"xfull": xs})
    dfull = np.empty((B, T, D), dtype=np.float32)
    for cc in range(NCORES):
        dfull[:, cc * 256:(cc + 1) * 256, :] = dslices[cc].reshape(B, 256, D)
    dsh = [np.ascontiguousarray(dfull.reshape(B * T, D)[cc * TOK:(cc + 1) * TOK])
           for cc in range(NCORES)]
    shards = run("ffn1d", "ffn1", {"x_in": shards, "d_in": dsh})
    shards = run("l1b", "l1b", {"x_in": shards})
    shards = run("ffn1", "ffn1", {"x_in": shards})
    return np.concatenate(shards, axis=0).reshape(B, T, D)

